# version 09
# baseline (speedup 1.0000x reference)
import contextlib
import numpy as np
import ml_dtypes
import concourse.bass as bass
import concourse.mybir as mybir
from concourse.bass_utils import run_bass_kernel_spmd

F32 = mybir.dt.float32
BF16 = mybir.dt.bfloat16
I32 = mybir.dt.int32
AF = mybir.ActivationFunctionType
ALU = mybir.AluOpType
AX = mybir.AxisListType

B_, S, DM = 4, 8192, 1024
DT_RANK, D_STATE = 64, 16
IF = 2816
SH = 512
TOK = S // 2
TC1 = 512
NC1 = S // TC1
TC2 = 512
NC2 = TOK // TC2
TSB = 1024
NCB = S // TSB
SEGW = TSB + 1
QT2 = [0.25, 2.25, 6.25, 12.25, 25.0, 49.0, 100.0]
LN12 = float(np.log(12.0))
NWT = 22
NDT = 4
NKD = DM // 128
NKI = IF // 128

_NC_CACHE = {}
DEBUG_STOP = None


def build():
    nc = bass.Bass("TRN2", target_bir_lowering=False, detect_race_conditions=False)

    def inp(name, shape, dt=F32):
        return nc.dram_tensor(name, shape, dt, kind="ExternalInput").ap()

    x = inp("x", [S, DM])
    x_dh = inp("x_dh", [S, SH])
    x_mine = inp("x_mine", [TOK, DM])
    wxT = inp("wxT", [DM, 96])
    wdtT_h = inp("wdtT_h", [DT_RANK, SH])
    bdt_h = inp("bdt_h", [128, 4])
    alog_h = inp("alog_h", [128, 64])
    dvec_h = inp("dvec_h", [128, 4])
    bc_sel = inp("bc_sel", [32, 32 * 128])
    idn = inp("idn", [128, 128])
    wshare = inp("wshare", [NWT * 128, DM])
    wdshare = inp("wdshare", [NDT * 128, IF])
    jsc = inp("jsc", [1, 4], I32)
    o = nc.dram_tensor("o", [S, SH], F32, kind="ExternalOutput").ap()

    def scr(name, shape, dt=BF16):
        return nc.dram_tensor(name, shape, dt, kind="Internal").ap()

    def shr(name, shape, dt=BF16):
        return nc.dram_tensor(name, shape, dt, kind="Internal",
                              addr_space="Shared").ap()

    xp_scr = scr("xp_scr", [96, S])
    xnscan = scr("xnscan", [DM, S])
    wguT = shr("wguT", [DM, 2 * IF])
    wdnT = shr("wdnT", [IF, DM])
    cc2sh = shr("cc2sh", [S, DM])
    fsh = shr("fsh", [2, 2], I32)
    g_my = scr("g_my", [S, SH])
    xn_my = scr("xn_my", [SH, S])

    es = contextlib.ExitStack()
    with es:
        sbuf = lambda n, sh, dt=F32: es.enter_context(nc.sbuf_tensor(n, sh, dt))
        psum = lambda n, sh, dt=F32: es.enter_context(nc.psum_tensor(n, sh, dt))
        sem = lambda n: es.enter_context(nc.semaphore(n))

        g_wxT = sbuf("g_wxT", [128, 8 * 96], BF16)
        g_wdtT = sbuf("g_wdtT", [64, SH], BF16)
        g_bdtn = sbuf("g_bdtn", [128, 4])
        g_expA = sbuf("g_expA", [128, 64])
        g_dvec = sbuf("g_dvec", [128, 4])
        g_sel = sbuf("g_sel", [32, 32 * 128], BF16)
        g_idn = sbuf("g_idn", [128, 128], BF16)
        g_jsc = sbuf("g_jsc", [1, 4], I32)
        g_fl = sbuf("g_fl", [1, 2], I32)
        g_zero = sbuf("g_zero", [128, 1])
        g_ln12 = sbuf("g_ln12", [128, 1])
        g_tiny = sbuf("g_tiny", [128, 1])
        g_eps = sbuf("g_eps", [128, 1])
        g_mhalf = sbuf("g_mhalf", [128, 1])
        g_state = sbuf("g_state", [128, 64])
        es_st = contextlib.ExitStack()
        g_stage = es_st.enter_context(nc.sbuf_tensor("g_stage", [128, 8 * 96], F32))
        g_stage2 = es_st.enter_context(nc.sbuf_tensor("g_stage2", [64, SH], F32))
        g_stage3 = es_st.enter_context(nc.sbuf_tensor("g_stage3", [32, 32 * 128], F32))
        g_stage4 = es_st.enter_context(nc.sbuf_tensor("g_stage4", [128, 128], F32))

        es_ps = contextlib.ExitStack()
        psp = lambda n, sh, dt=F32: es_ps.enter_context(nc.psum_tensor(n, sh, dt))

        qS = sem("qS")
        qP = sem("qP")
        qX = sem("qX")
        qW = sem("qW")
        qF = sem("qF")
        qG = sem("qG")
        sPE = sem("sPE")
        sACT = sem("sACT")
        sDVE = sem("sDVE")
        sPOOL = sem("sPOOL")
        cnt = {"qS": 0, "qP": 0, "qX": 0, "qW": 0,
               "PE": 0, "ACT": 0, "DVE": 0, "POOL": 0}
        SEMS = {"qS": qS, "qP": qP, "qX": qX, "qW": qW, "PE": sPE,
                "ACT": sACT, "DVE": sDVE, "POOL": sPOOL}

        def dma(eng, out, in_, q="qS"):
            cnt[q] += 16
            eng.dma_start(out=out, in_=in_).then_inc(SEMS[q], 16)
            return cnt[q]

        def mark(inst, e):
            cnt[e] += 1
            inst.then_inc(SEMS[e], 1)
            return cnt[e]

        block = es.enter_context(nc.Block())
        bar = sem("bar")
        bar_k = [0]

        def barrier():
            bar_k[0] += 1
            k = 5 * bar_k[0]

            @block.sync
            def _(sync):
                sync.sem_inc(bar, 1)
                sync.wait_ge(bar, k)

            @block.tensor
            def _(t):
                t.sem_inc(bar, 1)
                t.wait_ge(bar, k)

            @block.scalar
            def _(scalar):
                scalar.sem_inc(bar, 1)
                scalar.wait_ge(bar, k)

            @block.vector
            def _(v):
                v.sem_inc(bar, 1)
                v.wait_ge(bar, k)

            @block.gpsimd
            def _(g):
                g.sem_inc(bar, 1)
                g.wait_ge(bar, k)

        JREG = {}

        def jv(eng_name, eng):
            return eng.value_load(g_jsc[0:1, 0:1])

        def emit_poll(eng, etype, k, sem_q, name):
            tk = eng.value_load(g_jsc[0:1, 2 + k:3 + k])
            tq = nc.alloc_registers(f"tq_{name}", engines=[etype])
            nc.regs_mov(tq, 16)
            nc.br(f"poll_{name}", engines=[etype])
            with nc.body(f"poll_{name}", valid_engines=[etype]):
                eng.dma_start(out=g_fl[0:1, 0:2],
                              in_=fsh[k:k + 1, 0:2]).then_inc(sem_q, 16)
                eng.wait_ge(sem_q, tq)
                nc.regs_alu(tq, tq, 16, op=ALU.add)
                fr0 = eng.value_load(g_fl[0:1, 0:1])
                fr1 = eng.value_load(g_fl[0:1, 1:2])
                nc.br_cmp(fr0, tk, on_true=f"poll_{name}_c1",
                          on_false=f"poll_{name}", comp_op="IS_EQ",
                          engines=[etype])
            with nc.body(f"poll_{name}_c1", valid_engines=[etype]):
                nc.br_cmp(fr1, tk, on_true=f"poll_{name}_done",
                          on_false=f"poll_{name}", comp_op="IS_EQ",
                          engines=[etype])
            nc.switch_bb(f"poll_{name}_done")

        @block.sync
        def _(sync):
            dma(sync, g_stage[:].rearrange("p (a n) -> p a n", n=96),
                wxT.rearrange("(a p) n -> p a n", p=128))
            dma(sync, g_stage2[:], wdtT_h[:])
            dma(sync, g_bdtn[:], bdt_h[:])
            dma(sync, g_expA[:], alog_h[:])
            dma(sync, g_dvec[:], dvec_h[:])
            dma(sync, g_stage3[:], bc_sel[:])
            dma(sync, g_jsc[:], jsc[:])
            dma(sync, g_stage4[:], idn[:])
        ld0 = cnt["qS"]

        @block.vector
        def _(v):
            v.wait_ge(qS, ld0)
            v.tensor_copy(g_wxT[:], g_stage[:])
            v.tensor_copy(g_wdtT[:], g_stage2[:])
            v.tensor_copy(g_sel[:], g_stage3[:])
            v.tensor_copy(g_idn[:], g_stage4[:])
            v.tensor_scalar_mul(g_bdtn[:], g_bdtn[:], -1.0)
            v.memset(g_zero[:], 0.0)
            v.memset(g_ln12[:], LN12)
            v.memset(g_tiny[:], 1e-30)
            v.memset(g_eps[:], 1e-5)
            v.memset(g_mhalf[:], -0.5)
            mark(v.memset(g_state[:], 0.0), "DVE")

        @block.scalar
        def _(scalar):
            scalar.wait_ge(qS, ld0)
            mark(scalar.activation(out=g_expA[:], in_=g_expA[:], func=AF.Exp), "ACT")

        barrier()
        es_st.close()
        if DEBUG_STOP == "W0":
            return nc

        es_aq = contextlib.ExitStack()
        es_ph = contextlib.ExitStack()
        sbq = lambda n, sh, dt=F32: es_aq.enter_context(nc.sbuf_tensor(n, sh, dt))
        sbp = lambda n, sh, dt=F32: es_ph.enter_context(nc.sbuf_tensor(n, sh, dt))
        QW = 1408
        q_t = sbq("q_t", [128, QW])
        q_u = sbq("q_u", [128, QW])
        q_m = [sbq(f"q_m{k}", [128, QW], BF16) for k in range(7)]
        q_a = sbq("q_a", [128, QW], BF16)
        q_b = sbq("q_b", [128, QW], BF16)
        q_bm = sbq("q_bm", [128, QW // 16])
        q_rs = sbq("q_rs", [128, QW // 16])
        q_sb = sbq("q_sb", [128, QW // 16])
        ptr = psp("ptr_w", [128, 1024], BF16)
        w_ld = sbp("w_ld", [128, IF])
        w_q = sbp("w_q", [128, IF], BF16)
        w_gsT = sbp("w_gsT", [128, NKD * NWT * 128], BF16)
        w_dsT = sbp("w_dsT", [128, NKI * NDT * 128], BF16)


        def emit_quant(src, dst, W):
            nb = W // 16
            t3 = lambda ap: ap.rearrange("p (b i) -> p b i", i=16)

            @block.vector
            def _(v):
                v.tensor_reduce(
                    q_bm[:, 0:nb].rearrange("p (b o) -> p b o", o=1),
                    t3(src), AX.X, ALU.max, apply_absolute_value=True)
                mark(v.tensor_scalar_max(q_bm[:, 0:nb], q_bm[:, 0:nb], 6e-12), "DVE")
            w0 = cnt["DVE"]

            @block.scalar
            def _(scalar):
                scalar.wait_ge(sDVE, w0)
                scalar.activation(out=q_rs[:, 0:nb], in_=q_bm[:, 0:nb], func=AF.Ln)
                mark(scalar.activation(out=q_rs[:, 0:nb], in_=q_rs[:, 0:nb],
                                       func=AF.Exp, scale=-1.0, bias=g_ln12[:, 0:1]), "ACT")
            a0 = cnt["ACT"]

            @block.vector
            def _(v):
                v.wait_ge(sACT, a0)
                v.tensor_scalar_mul(q_sb[:, 0:nb], q_bm[:, 0:nb], 1.0 / 12.0)
                bc = lambda ap: ap.rearrange("p (b o) -> p b o", o=1) \
                                  .broadcast_to([128, nb, 16])
                v.tensor_tensor(t3(q_t[:, 0:W]), t3(src), bc(q_rs[:, 0:nb]), ALU.mult)
                v.tensor_mul(q_u[:, 0:W], q_t[:, 0:W], q_t[:, 0:W])
                for k in range(7):
                    v.tensor_single_scalar(q_m[k][:, 0:W], q_u[:, 0:W], QT2[k], ALU.is_gt)
                v.tensor_add(q_a[:, 0:W], q_m[0][:, 0:W], q_m[1][:, 0:W])
                v.tensor_add(q_b[:, 0:W], q_m[2][:, 0:W], q_m[3][:, 0:W])
                v.tensor_add(q_a[:, 0:W], q_a[:, 0:W], q_b[:, 0:W])
                v.tensor_add(q_b[:, 0:W], q_m[4][:, 0:W], q_m[5][:, 0:W])
                v.scalar_tensor_tensor(q_a[:, 0:W], q_b[:, 0:W], 2.0, q_a[:, 0:W],
                                       ALU.mult, ALU.add)
                v.scalar_tensor_tensor(q_a[:, 0:W], q_m[6][:, 0:W], 4.0, q_a[:, 0:W],
                                       ALU.mult, ALU.add)
                v.tensor_scalar(q_b[:, 0:W], q_t[:, 0:W], 0.0, 2.0, ALU.is_ge, ALU.mult)
                v.tensor_mul(q_m[0][:, 0:W], q_a[:, 0:W], q_b[:, 0:W])
                v.tensor_sub(q_a[:, 0:W], q_m[0][:, 0:W], q_a[:, 0:W])
                mark(v.tensor_tensor(t3(dst), t3(q_a[:, 0:W]), bc(q_sb[:, 0:nb]),
                                     ALU.mult), "DVE")

        def emit_stats(xsrc, ntile, sbn, smv, smean, svar, srstd):
            @block.vector
            def _(v):
                last = None
                for a in range(ntile):
                    for h in range(2):
                        last = v.bn_stats(sbn[:, (a * 2 + h) * 6:(a * 2 + h) * 6 + 6],
                                          xsrc[:, a * 1024 + h * 512:a * 1024 + h * 512 + 512])
                mark(last, "DVE")
                v.wait_ge(sDVE, cnt["DVE"])
                for a in range(ntile):
                    last = v.bn_aggr(smv[:, a * 2:a * 2 + 2], sbn[:, a * 12:a * 12 + 12])
                mark(last, "DVE")
                v.wait_ge(sDVE, cnt["DVE"])
                mv2 = smv[:, 0:2 * ntile].rearrange("p (a two) -> p a two", two=2)
                v.tensor_copy(smean[:, 0:ntile].rearrange("p (a o) -> p a o", o=1),
                              mv2[:, :, 0:1])
                mark(v.tensor_copy(svar[:, 0:ntile].rearrange("p (a o) -> p a o", o=1),
                                   mv2[:, :, 1:2]), "DVE")
            d1 = cnt["DVE"]

            @block.scalar
            def _(scalar):
                scalar.wait_ge(sDVE, d1)
                scalar.activation(out=srstd[:, 0:ntile], in_=svar[:, 0:ntile],
                                  func=AF.Ln, bias=g_eps[:, 0:1])
                mark(scalar.activation(out=srstd[:, 0:ntile], in_=srstd[:, 0:ntile],
                                       func=AF.Exp, scale=g_mhalf[:, 0:1]), "ACT")

        for ti in range(NWT):
            @block.sync
            def _(sync, ti=ti):
                if ti == 0:
                    JREG["wsp"] = jv("wsp", sync)
                sync.wait_ge(sDVE, cnt["DVE"])
                dma(sync, w_ld[:, 0:DM], wshare[ti * 128:(ti + 1) * 128, :])
            need = cnt["qS"]

            @block.vector
            def _(v, need=need):
                v.wait_ge(qS, need)
            emit_quant(w_ld[:, 0:DM], w_q[:, 0:DM], DM)
            dq = cnt["DVE"]

            @block.tensor
            def _(t, ti=ti, dq=dq):
                t.wait_ge(sDVE, dq)
                last = None
                for kd in range(NKD):
                    last = t.transpose(ptr[:, kd * 128:(kd + 1) * 128],
                                       w_q[:, kd * 128:(kd + 1) * 128], g_idn[:])
                mark(last, "PE")
            pq = cnt["PE"]

            @block.vector
            def _(v, ti=ti, pq=pq):
                v.wait_ge(sPE, pq)
                mark(v.tensor_copy(
                    w_gsT[:].rearrange("p (kd s c) -> p kd s c", s=NWT, c=128)[:, :, ti, :],
                    ptr[:].rearrange("p (kd c) -> p kd c", c=128)), "DVE")
            evq = cnt["DVE"]

            @block.tensor
            def _(t, evq=evq):
                t.wait_ge(sDVE, evq)

        for m in range(NDT):
            @block.sync
            def _(sync, m=m):
                sync.wait_ge(sDVE, cnt["DVE"])
                dma(sync, w_ld[:], wdshare[m * 128:(m + 1) * 128, :])
            need = cnt["qS"]

            @block.vector
            def _(v, need=need):
                v.wait_ge(qS, need)
            emit_quant(w_ld[:, 0:1408], w_q[:, 0:1408], 1408)
            emit_quant(w_ld[:, 1408:2816], w_q[:, 1408:2816], 1408)
            dq = cnt["DVE"]
            for rr in range(3):
                lo = rr * 8
                hi = min(lo + 8, NKI)

                @block.tensor
                def _(t, lo=lo, hi=hi, dq=dq):
                    t.wait_ge(sDVE, cnt["DVE"])
                    t.wait_ge(sDVE, dq)
                    last = None
                    for ic in range(lo, hi):
                        last = t.transpose(ptr[:, (ic - lo) * 128:(ic - lo + 1) * 128],
                                           w_q[:, ic * 128:(ic + 1) * 128], g_idn[:])
                    mark(last, "PE")
                pq = cnt["PE"]

                @block.vector
                def _(v, lo=lo, hi=hi, m=m, pq=pq):
                    v.wait_ge(sPE, pq)
                    mark(v.tensor_copy(
                        w_dsT[:].rearrange("p (ic mm c) -> p ic mm c",
                                           mm=NDT, c=128)[:, lo:hi, m, :],
                        ptr[:, 0:(hi - lo) * 128].rearrange("p (ic c) -> p ic c",
                                                            c=128)), "DVE")

        @block.sync
        def _(sync):
            sync.wait_ge(sDVE, cnt["DVE"])
            jr = JREG["wsp"]
            dma(sync, wguT.rearrange("(kd p) (s f) -> p kd s f",
                                     p=128, s=2)[:, :, bass.ts(jr, 1), :],
                w_gsT[:].rearrange("p (kd f) -> p kd f", f=NWT * 128))
            dma(sync, wdnT.rearrange("(ic p) (s f) -> p ic s f",
                                     p=128, s=2)[:, :, bass.ts(jr, 1), :],
                w_dsT[:].rearrange("p (ic f) -> p ic f", f=NDT * 128))
            sync.wait_ge(qS, cnt["qS"])
            dma(sync, fsh[0:1, bass.ts(jr, 1)], g_jsc[0:1, 2:3])

        barrier()

        if DEBUG_STOP == "W":
            es_ph.close()
            es_aq.close()
            es_ps.close()
            return nc

        es_ph.close()
        es_ps.close()
        es_ps = contextlib.ExitStack()
        psp = lambda n, sh, dt=F32: es_ps.enter_context(nc.psum_tensor(n, sh, dt))
        pa = psp("pa", [128, 512])
        pb = psp("pb", [128, 512])
        pc = psp("pc", [128, 512])
        pd = psp("pd", [128, 512])
        ptr = psp("ptr_a", [128, 1024], BF16)
        es_ph = contextlib.ExitStack()
        sbp = lambda n, sh, dt=F32: es_ph.enter_context(nc.sbuf_tensor(n, sh, dt))
        a_x = sbp("a_x", [128, 4 * 1024])
        a_xn = sbp("a_xn", [128, 4 * 1024])
        a_xnb = sbp("a_xnb", [128, 4 * 1024], BF16)
        a_fm = sbp("a_fm", [128, 8 * 512], BF16)
        a_xps = sbp("a_xps", [96, 512], BF16)
        a_bn = sbp("a_bn", [128, 48])
        a_mv = sbp("a_mv", [128, 8])
        a_mean = sbp("a_mean", [128, 4])
        a_var = sbp("a_var", [128, 4])
        a_rstd = sbp("a_rstd", [128, 4])

        for c in range(NC1):
            R0 = c * TC1

            @block.sync
            def _(sync, R0=R0):
                dma(sync, a_x[:].rearrange("p (a d) -> p a d", d=DM),
                    x[R0:R0 + TC1, :].rearrange("(a p) d -> p a d", p=128),
                    q="qX")
            ldx = cnt["qX"]
            spw = cnt["qS"]

            @block.vector
            def _(v, ldx=ldx, spw=spw):
                v.wait_ge(qX, ldx)
                v.wait_ge(qS, spw)
            emit_stats(a_x, 4, a_bn, a_mv, a_mean, a_var, a_rstd)
            a2 = cnt["ACT"]

            @block.vector
            def _(v, a2=a2):
                v.wait_ge(sACT, a2)
                for a in range(4):
                    v.tensor_scalar(a_xn[:, a * 1024:(a + 1) * 1024],
                                    a_x[:, a * 1024:(a + 1) * 1024],
                                    a_mean[:, a:a + 1], a_rstd[:, a:a + 1],
                                    ALU.subtract, ALU.mult)
                mark(v.tensor_copy(a_xnb[:], a_xn[:]), "DVE")
            dxn = cnt["DVE"]

            for kd in range(NKD):
                @block.tensor
                def _(t, kd=kd, dxn=dxn):
                    t.wait_ge(sDVE, cnt["DVE"])
                    t.wait_ge(sDVE, dxn)
                    last = None
                    for a in range(4):
                        last = t.transpose(ptr[:, a * 128:(a + 1) * 128],
                                           a_xnb[:, a * 1024 + kd * 128: a * 1024 + (kd + 1) * 128],
                                           g_idn[:])
                    mark(last, "PE")
                ptx = cnt["PE"]

                @block.vector
                def _(v, kd=kd, ptx=ptx):
                    v.wait_ge(sPE, ptx)
                    mark(v.tensor_copy(a_fm[:, kd * 512:(kd + 1) * 512], ptr[:, 0:512]),
                         "DVE")
            dfm = cnt["DVE"]

            @block.tensor
            def _(t, dfm=dfm):
                t.wait_ge(sDVE, dfm)
                last = None
                for k in range(NKD):
                    last = t.matmul(pa[0:96, :], g_wxT[:, k * 96:(k + 1) * 96],
                                    a_fm[:, k * 512:(k + 1) * 512],
                                    start=(k == 0), stop=(k == NKD - 1))
                mark(last, "PE")
            pxp = cnt["PE"]

            @block.scalar
            def _(scalar, pxp=pxp, spw=spw):
                scalar.wait_ge(sPE, pxp)
                scalar.wait_ge(qS, spw)
                mark(scalar.activation(out=a_xps[:], in_=pa[0:96, :], func=AF.Copy),
                     "ACT")
            axp = cnt["ACT"]

            @block.sync
            def _(sync, R0=R0, axp=axp, dfm=dfm):
                sync.wait_ge(sACT, axp)
                dma(sync, xp_scr[:, R0:R0 + TC1], a_xps[:])
                sync.wait_ge(sDVE, dfm)
                dma(sync, xnscan[:, R0:R0 + TC1].rearrange("(g p) t -> p g t", p=128),
                    a_fm[:].rearrange("p (g t) -> p g t", t=512))

            barrier()

        if DEBUG_STOP == "A1":
            es_ph.close()
            es_aq.close()
            es_ps.close()
            return nc

        es_ph.close()
        es_ph = contextlib.ExitStack()
        sbp = lambda n, sh, dt=F32: es_ph.enter_context(nc.sbuf_tensor(n, sh, dt))
        f_x = sbp("f_x", [128, 4 * 1024])
        f_xn = sbp("f_xn", [128, 1024])
        f_xq = sbp("f_xq", [128, 4 * 1024], BF16)
        f_qfm = sbp("f_qfm", [128, 8 * 512], BF16)
        f_lwg = sbp("f_lwg", [128, 2 * 1024], BF16)
        f_lwu = sbp("f_lwu", [128, 2 * 1024], BF16)
        f_sg = sbp("f_sg", [128, 512], BF16)
        f_ub = sbp("f_ub", [128, 512], BF16)
        f_hfm = sbp("f_hfm", [128, 512], BF16)
        f_htm = sbp("f_htm", [128, 4 * IF], BF16)
        f_hqtm = sbp("f_hqtm", [128, 4 * IF], BF16)
        f_hqfm = sbp("f_hqfm", [128, NKI * 512], BF16)
        f_wdn = sbp("f_wdn", [128, 2 * 512], BF16)
        f_g = sbp("f_g", [128, 4 * 1024], BF16)
        f_bn = sbp("f_bn", [128, 48])
        f_mv = sbp("f_mv", [128, 8])
        f_mean = sbp("f_mean", [128, 4])
        f_var = sbp("f_var", [128, 4])
        f_rstd = sbp("f_rstd", [128, 4])

        @block.sync
        def _(sync):
            emit_poll(sync, mybir.EngineType.SP, 0, qF, "W")
            JREG["sp_j"] = jv("sp", sync)

        for c in range(NC2):
            R0 = c * TC2

            @block.sync
            def _(sync, R0=R0):
                dma(sync, f_x[:].rearrange("p (a d) -> p a d", d=DM),
                    x_mine[R0:R0 + TC2, :].rearrange("(a p) d -> p a d", p=128),
                    q="qX")
            ldx = cnt["qX"]

            @block.vector
            def _(v, ldx=ldx):
                v.wait_ge(qX, ldx)
            emit_stats(f_x, 4, f_bn, f_mv, f_mean, f_var, f_rstd)
            a2 = cnt["ACT"]

            for a in range(4):
                @block.vector
                def _(v, a=a, a2=a2):
                    v.wait_ge(sACT, a2)
                    mark(v.tensor_scalar(f_xn[:], f_x[:, a * 1024:(a + 1) * 1024],
                                         f_mean[:, a:a + 1], f_rstd[:, a:a + 1],
                                         ALU.subtract, ALU.mult), "DVE")
                emit_quant(f_xn[:], f_xq[:, a * 1024:(a + 1) * 1024], 1024)
            dq2 = cnt["DVE"]

            for kd in range(NKD):
                @block.tensor
                def _(t, kd=kd, dq2=dq2):
                    t.wait_ge(sDVE, cnt["DVE"])
                    t.wait_ge(sDVE, dq2)
                    last = None
                    for a in range(4):
                        last = t.transpose(ptr[:, a * 128:(a + 1) * 128],
                                           f_xq[:, a * 1024 + kd * 128: a * 1024 + (kd + 1) * 128],
                                           g_idn[:])
                    mark(last, "PE")
                ptx = cnt["PE"]

                @block.vector
                def _(v, kd=kd, ptx=ptx):
                    v.wait_ge(sPE, ptx)
                    mark(v.tensor_copy(f_qfm[:, kd * 512:(kd + 1) * 512], ptr[:, 0:512]),
                         "DVE")
            dqf = cnt["DVE"]

            evac_gate = {}
            evac_up = {}
            mid_evac = {}
            for ic in range(NKI):
                par = ic % 2
                pg = pa if par == 0 else pb
                pu = pc if par == 0 else pd

                @block.sync
                def _(sync, ic=ic, par=par):
                    if ic >= 2:
                        sync.wait_ge(sPE, evac_gate[ic - 2][0])
                    dma(sync, f_lwg[:, par * 1024:(par + 1) * 1024].rearrange("p (k i) -> p k i", i=128),
                        wguT[:, ic * 128:(ic + 1) * 128].rearrange("(k p) i -> p k i", p=128),
                        q="qW")
                    dma(sync, f_lwu[:, par * 1024:(par + 1) * 1024].rearrange("p (k i) -> p k i", i=128),
                        wguT[:, IF + ic * 128:IF + (ic + 1) * 128].rearrange("(k p) i -> p k i", p=128),
                        q="qW")
                lw = cnt["qW"]

                @block.tensor
                def _(t, ic=ic, par=par, lw=lw, dqf=dqf, pg=pg, pu=pu):
                    t.wait_ge(qW, lw)
                    t.wait_ge(sDVE, dqf)
                    if ic >= 2:
                        t.wait_ge(sACT, evac_gate[ic - 2][1])
                    for k in range(NKD):
                        t.matmul(pg[:], f_lwg[:, par * 1024 + k * 128: par * 1024 + (k + 1) * 128],
                                 f_qfm[:, k * 512:(k + 1) * 512],
                                 start=(k == 0), stop=(k == NKD - 1))
                    last = None
                    for k in range(NKD):
                        last = t.matmul(pu[:], f_lwu[:, par * 1024 + k * 128: par * 1024 + (k + 1) * 128],
                                        f_qfm[:, k * 512:(k + 1) * 512],
                                        start=(k == 0), stop=(k == NKD - 1))
                    mark(last, "PE")
                mmc = cnt["PE"]
                evac_gate[ic] = (mmc, None)

                @block.scalar
                def _(scalar, ic=ic, mmc=mmc, pg=pg, pu=pu):
                    scalar.wait_ge(sPE, mmc)
                    scalar.wait_ge(sDVE, mid_evac.get(ic - 1, 0))
                    scalar.activation(out=f_sg[:], in_=pg[:], func=AF.Silu)
                    mark(scalar.activation(out=f_ub[:], in_=pu[:], func=AF.Copy), "ACT")
                evac_gate[ic] = (mmc, cnt["ACT"])

                @block.vector
                def _(v, ic=ic):
                    v.wait_ge(sACT, evac_gate[ic][1])
                    mark(v.tensor_mul(f_hfm[:], f_sg[:], f_ub[:]), "DVE")
                hfm = cnt["DVE"]
                mid_evac[ic] = hfm

                @block.tensor
                def _(t, ic=ic, hfm=hfm):
                    t.wait_ge(sDVE, hfm)
                    t.wait_ge(sDVE, cnt["DVE"])
                    last = None
                    for tt in range(4):
                        last = t.transpose(ptr[:, tt * 128:(tt + 1) * 128],
                                           f_hfm[:, tt * 128:(tt + 1) * 128], g_idn[:])
                    mark(last, "PE")
                ptm = cnt["PE"]

                @block.vector
                def _(v, ic=ic, ptm=ptm):
                    v.wait_ge(sPE, ptm)
                    mark(v.tensor_copy(
                        f_htm[:].rearrange("p (tt k c) -> p tt k c", k=NKI, c=128)[:, :, ic, :],
                        ptr[:, 0:512].rearrange("p (tt c) -> p tt c", c=128)), "DVE")
                mid_evac[ic] = cnt["DVE"]

            for tt in range(4):
                for piece, (off, W) in enumerate([(0, 1408), (1408, 1408)]):
                    emit_quant(f_htm[:, tt * IF + off: tt * IF + off + W],
                               f_hqtm[:, tt * IF + off: tt * IF + off + W], W)
            dhq = cnt["DVE"]

            for ic in range(NKI):
                @block.tensor
                def _(t, ic=ic, dhq=dhq):
                    t.wait_ge(sDVE, dhq)
                    t.wait_ge(sDVE, cnt["DVE"])
                    last = None
                    for tt in range(4):
                        last = t.transpose(ptr[:, tt * 128:(tt + 1) * 128],
                                           f_hqtm[:, tt * IF + ic * 128: tt * IF + (ic + 1) * 128],
                                           g_idn[:])
                    mark(last, "PE")
                ptq = cnt["PE"]

                @block.vector
                def _(v, ic=ic, ptq=ptq):
                    v.wait_ge(sPE, ptq)
                    mark(v.tensor_copy(f_hqfm[:, ic * 512:(ic + 1) * 512], ptr[:, 0:512]),
                         "DVE")
            dqfm = cnt["DVE"]

            PDN = [pa, pb, pc, pd]
            last_dn_evac = cnt["ACT"]
            spw_fg = cnt["qS"]
            for dh in range(2):
                for k in range(NKI):
                    par = k % 2

                    @block.sync
                    def _(sync, dh=dh, k=k, par=par):
                        if k >= 2 or dh > 0:
                            sync.wait_ge(sPE, cnt["PE"])
                        dma(sync, f_wdn[:, par * 512:(par + 1) * 512],
                            wdnT[k * 128:(k + 1) * 128, dh * 512:(dh + 1) * 512],
                            q="qW")
                    lwd = cnt["qW"]

                    @block.tensor
                    def _(t, dh=dh, k=k, par=par, lwd=lwd, dqfm=dqfm,
                          lde=last_dn_evac):
                        t.wait_ge(qW, lwd)
                        t.wait_ge(sDVE, dqfm)
                        if k == 0:
                            t.wait_ge(sACT, lde)
                        last = None
                        for tt in range(4):
                            last = t.matmul(PDN[tt][:],
                                            f_hqfm[:, k * 512 + tt * 128: k * 512 + (tt + 1) * 128],
                                            f_wdn[:, par * 512:(par + 1) * 512],
                                            start=(k == 0), stop=(k == NKI - 1))
                        mark(last, "PE")
                dnm = cnt["PE"]

                @block.scalar
                def _(scalar, dh=dh, dnm=dnm, spw=spw_fg):
                    scalar.wait_ge(sPE, dnm)
                    scalar.wait_ge(qS, spw)
                    last = None
                    for tt in range(4):
                        last = scalar.activation(out=f_g[:, tt * 1024 + dh * 512: tt * 1024 + (dh + 1) * 512],
                                                 in_=PDN[tt][:], func=AF.Silu)
                    mark(last, "ACT")
                last_dn_evac = cnt["ACT"]

            @block.sync
            def _(sync, c=c, lde=last_dn_evac):
                sync.wait_ge(sACT, cnt["ACT"])
                dma(sync, cc2sh.rearrange("(s cc a p) d -> p s cc a d",
                                          s=2, cc=NC2, p=128)
                    [:, bass.ts(JREG["sp_j"], 1), c, :, :],
                    f_g[:].rearrange("p (a d) -> p a d", d=DM))

            barrier()

        @block.sync
        def _(sync):
            sync.wait_ge(qS, cnt["qS"])
            dma(sync, fsh[1:2, bass.ts(JREG["sp_j"], 1)], g_jsc[0:1, 3:4])

        @block.gpsimd
        def _(g):
            JREG["gp_j"] = jv("gp", g)
            g.wait_ge(qS, cnt["qS"])
            emit_poll(g, mybir.EngineType.Pool, 1, qG, "G")
            dma(g, g_my[:], cc2sh[:, bass.ts(JREG["gp_j"], 512)], q="qP")
            dma(g, xn_my[:], xnscan[bass.ts(JREG["gp_j"], 512), :], q="qP")
            g.wait_ge(qP, cnt["qP"])

        barrier()

        if DEBUG_STOP == "A2":
            es_ph.close()
            es_aq.close()
            es_ps.close()
            return nc

        es_ph.close()
        es_aq.close()
        es_ps.close()
        es_ps = contextlib.ExitStack()
        psp = lambda n, sh, dt=F32: es_ps.enter_context(nc.psum_tensor(n, sh, dt))
        pq0 = psp("pq0", [128, 1024])
        pq1 = psp("pq1", [128, 1024])
        ptr = psp("ptr_b", [128, 512], BF16)
        es_ph = contextlib.ExitStack()
        sbp = lambda n, sh, dt=F32: es_ph.enter_context(nc.sbuf_tensor(n, sh, dt))
        b_xn = sbp("b_xn", [128, 4 * 1024], BF16)
        b_xp = sbp("b_xp", [96, 1024], BF16)
        b_xp2 = sbp("b_xp2", [32, 1024], BF16)
        b_xdh = sbp("b_xdh", [128, 8 * 512])
        b_g = sbp("b_g", [128, 8 * 512], BF16)
        b_sig = sbp("b_sig", [128, 1024])
        b_dt = sbp("b_dt", [128, 4 * 1024], BF16)
        b_dx = sbp("b_dx", [128, 4 * 1024], BF16)
        b_B = sbp("b_B", [128, 8 * 1024], BF16)
        b_C = sbp("b_C", [128, 8 * 1024], BF16)
        b_a0 = sbp("b_a0", [128, 8 * SEGW], BF16)
        b_a1 = sbp("b_a1", [128, 8 * SEGW], BF16)
        b_u = sbp("b_u", [128, 8 * SEGW], BF16)
        b_h = sbp("b_h", [128, 8 * SEGW], BF16)
        b_t1 = sbp("b_t1", [128, 4 * 1024], BF16)
        b_t2 = sbp("b_t2", [128, 2 * 1024], BF16)
        b_t3 = sbp("b_t3", [128, 1024])
        b_y = sbp("b_y", [128, 4 * 1024])
        b_s = sbp("b_s", [128, 4 * 1024], BF16)
        b_stm = sbp("b_stm", [128, 512], BF16)
        b_o = sbp("b_o", [128, 512])

        @block.vector
        def _(v):
            v.memset(b_a0[:], 0.0)
            mark(v.memset(b_a1[:], 0.0), "DVE")

        barrier()

        AB = [b_a0, b_a1]
        u3 = b_u.rearrange("p (s l) -> p s l", l=SEGW)
        h3 = b_h.rearrange("p (s l) -> p s l", l=SEGW)
        scan_done = {}
        abuf_free = {}

        for cb in range(NCB):
            R0 = cb * TSB

            @block.sync
            def _(sync, R0=R0):
                dma(sync, b_xn[:].rearrange("p (g t) -> p g t", t=TSB),
                    xn_my[:, R0:R0 + TSB].rearrange("(g p) t -> p g t", p=128))
                dma(sync, b_xp[:], xp_scr[:, R0:R0 + TSB])
                dma(sync, b_xp2[:], xp_scr[64:96, R0:R0 + TSB])
                dma(sync, b_xdh[:].rearrange("p (a d) -> p a d", d=SH),
                    x_dh[R0:R0 + TSB, :].rearrange("(a p) d -> p a d", p=128))
                dma(sync, b_g[:].rearrange("p (a d) -> p a d", d=SH),
                    g_my[R0:R0 + TSB, :].rearrange("(a p) d -> p a d", p=128))
            ldb = cnt["qS"]

            delta_evac = {}
            for g in range(4):
                pdx = pq0 if g % 2 == 0 else pq1

                @block.tensor
                def _(t, g=g, ldb=ldb, pdx=pdx):
                    t.wait_ge(qS, ldb)
                    if g >= 2:
                        t.wait_ge(sACT, delta_evac[g - 2])
                    t.matmul(pdx[:, 0:512], g_wdtT[:, g * 128:(g + 1) * 128],
                             b_xp[0:64, 0:512], start=True, stop=True)
                    mark(t.matmul(pdx[:, 512:1024], g_wdtT[:, g * 128:(g + 1) * 128],
                                  b_xp[0:64, 512:1024], start=True, stop=True), "PE")
                dmm = cnt["PE"]

                @block.scalar
                def _(scalar, g=g, dmm=dmm, pdx=pdx):
                    scalar.wait_ge(sPE, dmm)
                    scalar.activation(out=b_sig[:], in_=pdx[:], func=AF.Sigmoid,
                                      scale=-1.0, bias=g_bdtn[:, g:g + 1])
                    mark(scalar.activation(out=b_dt[:, g * 1024:(g + 1) * 1024],
                                           in_=b_sig[:], func=AF.Ln,
                                           bias=g_tiny[:, 0:1]),
                         "ACT")
                delta_evac[g] = cnt["ACT"]
            dt_done = cnt["ACT"]

            @block.vector
            def _(v, ldb=ldb, dt_done=dt_done):
                v.wait_ge(qS, ldb)
                v.wait_ge(sACT, dt_done)
                last = None
                for g in range(4):
                    last = v.tensor_mul(b_dx[:, g * 1024:(g + 1) * 1024],
                                        b_dt[:, g * 1024:(g + 1) * 1024],
                                        b_xn[:, g * 1024:(g + 1) * 1024])
                mark(last, "DVE")
            dx_done = cnt["DVE"]

            it = 0
            bc_psum_free = {id(pq0): ("ACT", delta_evac[2]),
                            id(pq1): ("ACT", delta_evac[3])}
            bci = 0
            for hh in range(2):
                for n in range(8):
                    nn = hh * 8 + n
                    for which, dst in ((0, b_B), (1, b_C)):
                        pq_ = pq0 if bci % 2 == 0 else pq1
                        prevkind, prevfree = bc_psum_free[id(pq_)]

                        @block.tensor
                        def _(t, nn=nn, which=which, pq_=pq_, ldb=ldb,
                              prevfree=prevfree, prevkind=prevkind):
                            t.wait_ge(qS, ldb)
                            t.wait_ge(SEMS[prevkind], prevfree)
                            sel = g_sel[:, (which * 16 + nn) * 128:(which * 16 + nn + 1) * 128]
                            t.matmul(pq_[:, 0:512], sel, b_xp2[:, 0:512],
                                     start=True, stop=True)
                            mark(t.matmul(pq_[:, 512:1024], sel, b_xp2[:, 512:1024],
                                          start=True, stop=True), "PE")
                        bmm = cnt["PE"]

                        if bci % 4 < 2:
                            @block.vector
                            def _(v, n=n, dst=dst, bmm=bmm, pq_=pq_):
                                v.wait_ge(sPE, bmm)
                                mark(v.tensor_copy(dst[:, n * 1024:(n + 1) * 1024],
                                                   pq_[:]), "DVE")
                            bc_psum_free[id(pq_)] = ("DVE", cnt["DVE"])
                        else:
                            @block.scalar
                            def _(scalar, n=n, dst=dst, bmm=bmm, pq_=pq_):
                                scalar.wait_ge(sPE, bmm)
                                mark(scalar.activation(
                                    out=dst[:, n * 1024:(n + 1) * 1024],
                                    in_=pq_[:], func=AF.Copy), "ACT")
                            bc_psum_free[id(pq_)] = ("ACT", cnt["ACT"])
                        bci += 1
                bc_done_d = cnt["DVE"]
                bc_done_a = cnt["ACT"]

                for g in range(4):
                    par = it % 2
                    abuf = AB[par]

                    @block.scalar
                    def _(scalar, g=g, hh=hh, par=par, abuf=abuf):
                        prev = abuf_free.get(par)
                        if prev is not None:
                            scalar.wait_ge(sDVE, prev)
                        scalar.wait_ge(sACT, dt_done)
                        last = None
                        for n in range(8):
                            last = scalar.activation(
                                out=abuf[:, n * SEGW + 1: n * SEGW + 1 + TSB],
                                in_=b_dt[:, g * 1024:(g + 1) * 1024], func=AF.Exp,
                                scale=g_expA[:, g * 16 + hh * 8 + n: g * 16 + hh * 8 + n + 1])
                        mark(last, "ACT")
                    aw = cnt["ACT"]

                    @block.vector
                    def _(v, g=g, hh=hh, par=par, abuf=abuf, aw=aw, dx_done=dx_done,
                          bc_done_d=bc_done_d, bc_done_a=bc_done_a):
                        v.wait_ge(sACT, max(aw, bc_done_a))
                        v.wait_ge(sDVE, max(dx_done, bc_done_d))
                        base = g * 16 + hh * 8
                        dxg = b_dx[:, g * 1024:(g + 1) * 1024] \
                            .rearrange("p (o t) -> p o t", o=1).broadcast_to([128, 8, TSB])
                        BH = b_B[:].rearrange("p (s t) -> p s t", t=1024)
                        CH = b_C[:].rearrange("p (s t) -> p s t", t=1024)
                        v.tensor_tensor(u3[:, :, 1:], dxg, BH, ALU.mult)
                        mark(v.tensor_copy(u3[:, :, 0:1],
                                           g_state[:, base:base + 8].rearrange("p (s o) -> p s o", o=1)),
                             "DVE")
                        v.wait_ge(sDVE, cnt["DVE"])
                        mark(v.tensor_tensor_scan(b_h[:], abuf[:], b_u[:],
                                                  g_zero[:, 0:1], ALU.mult, ALU.add),
                             "DVE")
                        abuf_free[par] = cnt["DVE"]
                        v.wait_ge(sDVE, cnt["DVE"])
                        v.tensor_copy(g_state[:, base:base + 8].rearrange("p (s o) -> p s o", o=1),
                                      h3[:, :, TSB:TSB + 1])
                        v.tensor_tensor(u3[:, :, 1:], h3[:, :, 1:], CH, ALU.mult)
                        v.tensor_tensor(b_t1[:].rearrange("p (s t) -> p s t", t=1024),
                                        u3[:, 0:4, 1:], u3[:, 4:8, 1:], ALU.add)
                        v.tensor_add(b_t2[:], b_t1[:, 0:2048], b_t1[:, 2048:4096])
                        if hh == 0:
                            mark(v.tensor_add(b_y[:, g * 1024:(g + 1) * 1024],
                                              b_t2[:, 0:1024], b_t2[:, 1024:2048]),
                                 "DVE")
                        else:
                            v.tensor_add(b_t3[:], b_t2[:, 0:1024], b_t2[:, 1024:2048])
                            mark(v.tensor_add(b_y[:, g * 1024:(g + 1) * 1024],
                                              b_y[:, g * 1024:(g + 1) * 1024],
                                              b_t3[:]), "DVE")
                    it += 1

            @block.vector
            def _(v):
                last = None
                for g in range(4):
                    last = v.scalar_tensor_tensor(b_s[:, g * 1024:(g + 1) * 1024],
                                                  b_xn[:, g * 1024:(g + 1) * 1024],
                                                  g_dvec[:, g:g + 1],
                                                  b_y[:, g * 1024:(g + 1) * 1024],
                                                  ALU.mult, ALU.subtract)
                mark(last, "DVE")
            sdone = cnt["DVE"]

            for tt in range(8):
                @block.tensor
                def _(t, tt=tt, sdone=sdone):
                    t.wait_ge(sDVE, sdone)
                    t.wait_ge(sDVE, cnt["DVE"])
                    last = None
                    for g in range(4):
                        last = t.transpose(ptr[:, g * 128:(g + 1) * 128],
                                           b_s[:, g * 1024 + tt * 128: g * 1024 + (tt + 1) * 128],
                                           g_idn[:])
                    mark(last, "PE")
                pts = cnt["PE"]

                @block.vector
                def _(v, tt=tt, pts=pts, pool_prev=cnt["POOL"]):
                    v.wait_ge(sPE, pts)
                    v.wait_ge(sPOOL, pool_prev)
                    mark(v.tensor_copy(b_stm[:], ptr[:, 0:512]), "DVE")
                stmd = cnt["DVE"]

                @block.gpsimd
                def _(g, tt=tt, stmd=stmd, ldb=ldb):
                    g.wait_ge(sDVE, stmd)
                    g.wait_ge(qS, ldb)
                    g.wait_ge(qP, cnt["qP"])
                    g.tensor_tensor(b_o[:], b_stm[:], b_g[:, tt * 512:(tt + 1) * 512],
                                    ALU.mult).then_inc(sPOOL, 1)
                    cnt["POOL"] += 1
                    g.wait_ge(sPOOL, cnt["POOL"])
                    mark(g.tensor_add(b_o[:], b_o[:], b_xdh[:, tt * 512:(tt + 1) * 512]),
                         "POOL")
                @block.gpsimd
                def _(g, tt=tt, R0=R0):
                    dma(g, o[R0 + tt * 128: R0 + (tt + 1) * 128, :], b_o[:], q="qP")

            barrier()

        @block.sync
        def _(sync):
            sync.wait_ge(qP, cnt["qP"])
            sync.wait_ge(qS, cnt["qS"])

        es_ph.close()
        es_ps.close()

    return nc


def kernel(L0_attn_x, L0_attn_W_xproj, L0_attn_W_dt, L0_attn_b_dt, L0_attn_A_log,
           L0_attn_D, L0_ffn_w_gate, L0_ffn_w_up, L0_ffn_w_down):
    x = np.ascontiguousarray(np.asarray(L0_attn_x, dtype=np.float32))
    W_xproj = np.asarray(L0_attn_W_xproj, dtype=np.float32)
    W_dt = np.asarray(L0_attn_W_dt, dtype=np.float32)
    b_dt = np.asarray(L0_attn_b_dt, dtype=np.float32)
    A_log = np.asarray(L0_attn_A_log, dtype=np.float32)
    D = np.asarray(L0_attn_D, dtype=np.float32)
    w_gate = np.asarray(L0_ffn_w_gate, dtype=np.float32)
    w_up = np.asarray(L0_ffn_w_up, dtype=np.float32)
    w_down = np.asarray(L0_ffn_w_down, dtype=np.float32)

    wxT = np.ascontiguousarray(W_xproj.T)
    sel = np.zeros((32, 32 * 128), np.float32)
    for n in range(32):
        sel[n, n * 128:(n + 1) * 128] = 1.0
    idn = np.eye(128, dtype=np.float32)

    if "nc" not in _NC_CACHE:
        _NC_CACHE["nc"] = build()
    nc = _NC_CACHE["nc"]

    rtok = np.random.randint(1 << 20, (1 << 30), size=2).astype(np.int32)

    in_maps = []
    for i in range(8):
        b, j = i // 2, i % 2
        sl = slice(j * SH, (j + 1) * SH)
        in_maps.append(dict(
            x=x[b],
            x_dh=np.ascontiguousarray(x[b][:, sl]),
            x_mine=np.ascontiguousarray(x[b][j * TOK:(j + 1) * TOK]),
            wxT=wxT,
            wdtT_h=np.ascontiguousarray(W_dt.T[:, sl]),
            bdt_h=np.ascontiguousarray(b_dt[sl].reshape(4, 128).T),
            alog_h=np.ascontiguousarray(
                A_log[sl].reshape(4, 128, 16).transpose(1, 0, 2).reshape(128, 64)),
            dvec_h=np.ascontiguousarray(D[sl].reshape(4, 128).T),
            bc_sel=sel,
            idn=idn,
            wshare=np.ascontiguousarray(w_gate if j == 0 else w_up),
            wdshare=np.ascontiguousarray(w_down[j * 512:(j + 1) * 512]),
            jsc=np.array([[j, 1 - j, rtok[0], rtok[1]]], np.int32),
        ))
    res = run_bass_kernel_spmd(nc, in_maps, core_ids=list(range(8)))
    out = np.empty((B_, S, DM), np.float32)
    for i in range(8):
        b, j = i // 2, i % 2
        out[b, :, j * SH:(j + 1) * SH] = np.asarray(res.results[i]["o"])
    return out


# version 10
# speedup vs baseline: 1.1321x; 1.1321x over previous
import contextlib
import numpy as np
import ml_dtypes
import concourse.bass as bass
import concourse.mybir as mybir
from concourse.bass_utils import run_bass_kernel_spmd

F32 = mybir.dt.float32
BF16 = mybir.dt.bfloat16
I32 = mybir.dt.int32
AF = mybir.ActivationFunctionType
ALU = mybir.AluOpType
AX = mybir.AxisListType

B_, S, DM = 4, 8192, 1024
DT_RANK, D_STATE = 64, 16
IF = 2816
SH = 512
TOK = S // 2
TC1 = 512
NC1 = S // TC1
TC2 = 512
NC2 = TOK // TC2
TSB = 1024
NCB = S // TSB
SEGW = TSB + 1
QT2 = [0.25, 2.25, 6.25, 12.25, 25.0, 49.0, 100.0]
LN12 = float(np.log(12.0))
NWT = 22
NDT = 4
NKD = DM // 128
NKI = IF // 128

_NC_CACHE = {}
DEBUG_STOP = None


def build():
    nc = bass.Bass("TRN2", target_bir_lowering=False, detect_race_conditions=False)

    def inp(name, shape, dt=F32):
        return nc.dram_tensor(name, shape, dt, kind="ExternalInput").ap()

    x = inp("x", [S, DM])
    x_dh = inp("x_dh", [S, SH])
    x_mine = inp("x_mine", [TOK, DM])
    wxT = inp("wxT", [DM, 96])
    wdtT_h = inp("wdtT_h", [DT_RANK, SH])
    bdt_h = inp("bdt_h", [128, 4])
    alog_h = inp("alog_h", [128, 64])
    dvec_h = inp("dvec_h", [128, 4])
    bc_sel = inp("bc_sel", [32, 32 * 128])
    idn = inp("idn", [128, 128])
    wshare = inp("wshare", [NWT * 128, DM])
    wdshare = inp("wdshare", [NDT * 128, IF])
    jsc = inp("jsc", [1, 4], I32)
    o = nc.dram_tensor("o", [S, SH], F32, kind="ExternalOutput").ap()

    def scr(name, shape, dt=BF16):
        return nc.dram_tensor(name, shape, dt, kind="Internal").ap()

    def shr(name, shape, dt=BF16):
        return nc.dram_tensor(name, shape, dt, kind="Internal",
                              addr_space="Shared").ap()

    xp_scr = scr("xp_scr", [96, S])
    xnscan = scr("xnscan", [DM, S])
    wguT = shr("wguT", [DM, 2 * IF])
    wdnT = shr("wdnT", [IF, DM])
    cc2sh = shr("cc2sh", [S, DM])
    fsh = shr("fsh", [2, 2], I32)
    g_my = scr("g_my", [S, SH])
    xn_my = scr("xn_my", [SH, S])

    es = contextlib.ExitStack()
    with es:
        sbuf = lambda n, sh, dt=F32: es.enter_context(nc.sbuf_tensor(n, sh, dt))
        psum = lambda n, sh, dt=F32: es.enter_context(nc.psum_tensor(n, sh, dt))
        sem = lambda n: es.enter_context(nc.semaphore(n))

        g_wxT = sbuf("g_wxT", [128, 8 * 96], BF16)
        g_wdtT = sbuf("g_wdtT", [64, SH], BF16)
        g_bdtn = sbuf("g_bdtn", [128, 4])
        g_expA = sbuf("g_expA", [128, 64])
        g_dvec = sbuf("g_dvec", [128, 4])
        g_sel = sbuf("g_sel", [32, 32 * 128], BF16)
        g_idn = sbuf("g_idn", [128, 128], BF16)
        g_jsc = sbuf("g_jsc", [1, 4], I32)
        g_fl = sbuf("g_fl", [1, 2], I32)
        g_zero = sbuf("g_zero", [128, 1])
        g_ln12 = sbuf("g_ln12", [128, 1])
        g_tiny = sbuf("g_tiny", [128, 1])
        g_eps = sbuf("g_eps", [128, 1])
        g_mhalf = sbuf("g_mhalf", [128, 1])
        g_state = sbuf("g_state", [128, 64])
        es_st = contextlib.ExitStack()
        g_stage = es_st.enter_context(nc.sbuf_tensor("g_stage", [128, 8 * 96], F32))
        g_stage2 = es_st.enter_context(nc.sbuf_tensor("g_stage2", [64, SH], F32))
        g_stage3 = es_st.enter_context(nc.sbuf_tensor("g_stage3", [32, 32 * 128], F32))
        g_stage4 = es_st.enter_context(nc.sbuf_tensor("g_stage4", [128, 128], F32))

        es_ps = contextlib.ExitStack()
        psp = lambda n, sh, dt=F32: es_ps.enter_context(nc.psum_tensor(n, sh, dt))

        qS = sem("qS")
        qP = sem("qP")
        qX = sem("qX")
        qW = sem("qW")
        qF = sem("qF")
        qG = sem("qG")
        sPE = sem("sPE")
        sACT = sem("sACT")
        sDVE = sem("sDVE")
        sPOOL = sem("sPOOL")
        cnt = {"qS": 0, "qP": 0, "qX": 0, "qW": 0,
               "PE": 0, "ACT": 0, "DVE": 0, "POOL": 0}
        SEMS = {"qS": qS, "qP": qP, "qX": qX, "qW": qW, "PE": sPE,
                "ACT": sACT, "DVE": sDVE, "POOL": sPOOL}

        def dma(eng, out, in_, q="qS"):
            cnt[q] += 16
            eng.dma_start(out=out, in_=in_).then_inc(SEMS[q], 16)
            return cnt[q]

        def mark(inst, e):
            cnt[e] += 1
            inst.then_inc(SEMS[e], 1)
            return cnt[e]

        block = es.enter_context(nc.Block())
        bar = sem("bar")
        bar_k = [0]

        def barrier():
            bar_k[0] += 1
            k = 5 * bar_k[0]

            @block.sync
            def _(sync):
                sync.sem_inc(bar, 1)
                sync.wait_ge(bar, k)

            @block.tensor
            def _(t):
                t.sem_inc(bar, 1)
                t.wait_ge(bar, k)

            @block.scalar
            def _(scalar):
                scalar.sem_inc(bar, 1)
                scalar.wait_ge(bar, k)

            @block.vector
            def _(v):
                v.sem_inc(bar, 1)
                v.wait_ge(bar, k)

            @block.gpsimd
            def _(g):
                g.sem_inc(bar, 1)
                g.wait_ge(bar, k)

        JREG = {}

        def jv(eng_name, eng):
            return eng.value_load(g_jsc[0:1, 0:1])

        def emit_poll(eng, etype, k, sem_q, name):
            tk = eng.value_load(g_jsc[0:1, 2 + k:3 + k])
            tq = nc.alloc_registers(f"tq_{name}", engines=[etype])
            nc.regs_mov(tq, 16)
            nc.br(f"poll_{name}", engines=[etype])
            with nc.body(f"poll_{name}", valid_engines=[etype]):
                eng.dma_start(out=g_fl[0:1, 0:2],
                              in_=fsh[k:k + 1, 0:2]).then_inc(sem_q, 16)
                eng.wait_ge(sem_q, tq)
                nc.regs_alu(tq, tq, 16, op=ALU.add)
                fr0 = eng.value_load(g_fl[0:1, 0:1])
                fr1 = eng.value_load(g_fl[0:1, 1:2])
                nc.br_cmp(fr0, tk, on_true=f"poll_{name}_c1",
                          on_false=f"poll_{name}", comp_op="IS_EQ",
                          engines=[etype])
            with nc.body(f"poll_{name}_c1", valid_engines=[etype]):
                nc.br_cmp(fr1, tk, on_true=f"poll_{name}_done",
                          on_false=f"poll_{name}", comp_op="IS_EQ",
                          engines=[etype])
            nc.switch_bb(f"poll_{name}_done")

        @block.sync
        def _(sync):
            dma(sync, g_stage[:].rearrange("p (a n) -> p a n", n=96),
                wxT.rearrange("(a p) n -> p a n", p=128))
            dma(sync, g_stage2[:], wdtT_h[:])
            dma(sync, g_bdtn[:], bdt_h[:])
            dma(sync, g_expA[:], alog_h[:])
            dma(sync, g_dvec[:], dvec_h[:])
            dma(sync, g_stage3[:], bc_sel[:])
            dma(sync, g_jsc[:], jsc[:])
            dma(sync, g_stage4[:], idn[:])
        ld0 = cnt["qS"]

        @block.vector
        def _(v):
            v.wait_ge(qS, ld0)
            v.tensor_copy(g_wxT[:], g_stage[:])
            v.tensor_copy(g_wdtT[:], g_stage2[:])
            v.tensor_copy(g_sel[:], g_stage3[:])
            v.tensor_copy(g_idn[:], g_stage4[:])
            v.tensor_scalar_mul(g_bdtn[:], g_bdtn[:], -1.0)
            v.memset(g_zero[:], 0.0)
            v.memset(g_ln12[:], LN12)
            v.memset(g_tiny[:], 1e-30)
            v.memset(g_eps[:], 1e-5)
            v.memset(g_mhalf[:], -0.5)
            mark(v.memset(g_state[:], 0.0), "DVE")

        @block.scalar
        def _(scalar):
            scalar.wait_ge(qS, ld0)
            mark(scalar.activation(out=g_expA[:], in_=g_expA[:], func=AF.Exp), "ACT")

        barrier()
        es_st.close()
        if DEBUG_STOP == "W0":
            return nc

        es_aq = contextlib.ExitStack()
        es_ph = contextlib.ExitStack()
        sbq = lambda n, sh, dt=F32: es_aq.enter_context(nc.sbuf_tensor(n, sh, dt))
        sbp = lambda n, sh, dt=F32: es_ph.enter_context(nc.sbuf_tensor(n, sh, dt))
        QW = 1408
        q_t = sbq("q_t", [128, QW])
        q_u = sbq("q_u", [128, QW])
        q_m = [sbq(f"q_m{k}", [128, QW], BF16) for k in range(7)]
        q_a = sbq("q_a", [128, QW], BF16)
        q_b = sbq("q_b", [128, QW], BF16)
        q_bm = sbq("q_bm", [128, QW // 16])
        q_rs = sbq("q_rs", [128, QW // 16])
        q_sb = sbq("q_sb", [128, QW // 16])
        ptr = psp("ptr_w", [128, 1024], BF16)
        w_ld = sbp("w_ld", [128, IF])
        w_q = sbp("w_q", [128, IF], BF16)
        w_gsT = sbp("w_gsT", [128, NKD * NWT * 128], BF16)
        w_dsT = sbp("w_dsT", [128, NKI * NDT * 128], BF16)


        def emit_quant(src, dst, W):
            nb = W // 16
            t3 = lambda ap: ap.rearrange("p (b i) -> p b i", i=16)

            @block.vector
            def _(v):
                v.tensor_reduce(
                    q_bm[:, 0:nb].rearrange("p (b o) -> p b o", o=1),
                    t3(src), AX.X, ALU.max, apply_absolute_value=True)
                mark(v.tensor_scalar_max(q_bm[:, 0:nb], q_bm[:, 0:nb], 6e-12), "DVE")
            w0 = cnt["DVE"]

            @block.scalar
            def _(scalar):
                scalar.wait_ge(sDVE, w0)
                scalar.activation(out=q_rs[:, 0:nb], in_=q_bm[:, 0:nb], func=AF.Ln)
                mark(scalar.activation(out=q_rs[:, 0:nb], in_=q_rs[:, 0:nb],
                                       func=AF.Exp, scale=-1.0, bias=g_ln12[:, 0:1]), "ACT")
            a0 = cnt["ACT"]

            @block.vector
            def _(v):
                v.wait_ge(sACT, a0)
                v.tensor_scalar_mul(q_sb[:, 0:nb], q_bm[:, 0:nb], 1.0 / 12.0)
                bc = lambda ap: ap.rearrange("p (b o) -> p b o", o=1) \
                                  .broadcast_to([128, nb, 16])
                v.tensor_tensor(t3(q_t[:, 0:W]), t3(src), bc(q_rs[:, 0:nb]), ALU.mult)
                v.tensor_mul(q_u[:, 0:W], q_t[:, 0:W], q_t[:, 0:W])
                for k in range(7):
                    v.tensor_single_scalar(q_m[k][:, 0:W], q_u[:, 0:W], QT2[k], ALU.is_gt)
                v.tensor_add(q_a[:, 0:W], q_m[0][:, 0:W], q_m[1][:, 0:W])
                v.tensor_add(q_b[:, 0:W], q_m[2][:, 0:W], q_m[3][:, 0:W])
                v.tensor_add(q_a[:, 0:W], q_a[:, 0:W], q_b[:, 0:W])
                v.tensor_add(q_b[:, 0:W], q_m[4][:, 0:W], q_m[5][:, 0:W])
                v.scalar_tensor_tensor(q_a[:, 0:W], q_b[:, 0:W], 2.0, q_a[:, 0:W],
                                       ALU.mult, ALU.add)
                v.scalar_tensor_tensor(q_a[:, 0:W], q_m[6][:, 0:W], 4.0, q_a[:, 0:W],
                                       ALU.mult, ALU.add)
                v.tensor_scalar(q_b[:, 0:W], q_t[:, 0:W], 0.0, 2.0, ALU.is_ge, ALU.mult)
                v.tensor_mul(q_m[0][:, 0:W], q_a[:, 0:W], q_b[:, 0:W])
                v.tensor_sub(q_a[:, 0:W], q_m[0][:, 0:W], q_a[:, 0:W])
                mark(v.tensor_tensor(t3(dst), t3(q_a[:, 0:W]), bc(q_sb[:, 0:nb]),
                                     ALU.mult), "DVE")

        def emit_stats(xsrc, ntile, sbn, smv, smean, svar, srstd):
            @block.vector
            def _(v):
                last = None
                for a in range(ntile):
                    for h in range(2):
                        last = v.bn_stats(sbn[:, (a * 2 + h) * 6:(a * 2 + h) * 6 + 6],
                                          xsrc[:, a * 1024 + h * 512:a * 1024 + h * 512 + 512])
                mark(last, "DVE")
                v.wait_ge(sDVE, cnt["DVE"])
                for a in range(ntile):
                    last = v.bn_aggr(smv[:, a * 2:a * 2 + 2], sbn[:, a * 12:a * 12 + 12])
                mark(last, "DVE")
                v.wait_ge(sDVE, cnt["DVE"])
                mv2 = smv[:, 0:2 * ntile].rearrange("p (a two) -> p a two", two=2)
                v.tensor_copy(smean[:, 0:ntile].rearrange("p (a o) -> p a o", o=1),
                              mv2[:, :, 0:1])
                mark(v.tensor_copy(svar[:, 0:ntile].rearrange("p (a o) -> p a o", o=1),
                                   mv2[:, :, 1:2]), "DVE")
            d1 = cnt["DVE"]

            @block.scalar
            def _(scalar):
                scalar.wait_ge(sDVE, d1)
                scalar.activation(out=srstd[:, 0:ntile], in_=svar[:, 0:ntile],
                                  func=AF.Ln, bias=g_eps[:, 0:1])
                mark(scalar.activation(out=srstd[:, 0:ntile], in_=srstd[:, 0:ntile],
                                       func=AF.Exp, scale=g_mhalf[:, 0:1]), "ACT")

        for ti in range(NWT):
            @block.sync
            def _(sync, ti=ti):
                if ti == 0:
                    JREG["wsp"] = jv("wsp", sync)
                sync.wait_ge(sDVE, cnt["DVE"])
                dma(sync, w_ld[:, 0:DM], wshare[ti * 128:(ti + 1) * 128, :])
            need = cnt["qS"]

            @block.vector
            def _(v, need=need):
                v.wait_ge(qS, need)
            emit_quant(w_ld[:, 0:DM], w_q[:, 0:DM], DM)
            dq = cnt["DVE"]

            @block.tensor
            def _(t, ti=ti, dq=dq):
                t.wait_ge(sDVE, dq)
                last = None
                for kd in range(NKD):
                    last = t.transpose(ptr[:, kd * 128:(kd + 1) * 128],
                                       w_q[:, kd * 128:(kd + 1) * 128], g_idn[:])
                mark(last, "PE")
            pq = cnt["PE"]

            @block.vector
            def _(v, ti=ti, pq=pq):
                v.wait_ge(sPE, pq)
                mark(v.tensor_copy(
                    w_gsT[:].rearrange("p (kd s c) -> p kd s c", s=NWT, c=128)[:, :, ti, :],
                    ptr[:].rearrange("p (kd c) -> p kd c", c=128)), "DVE")
            evq = cnt["DVE"]

            @block.tensor
            def _(t, evq=evq):
                t.wait_ge(sDVE, evq)

        for m in range(NDT):
            @block.sync
            def _(sync, m=m):
                sync.wait_ge(sDVE, cnt["DVE"])
                dma(sync, w_ld[:], wdshare[m * 128:(m + 1) * 128, :])
            need = cnt["qS"]

            @block.vector
            def _(v, need=need):
                v.wait_ge(qS, need)
            emit_quant(w_ld[:, 0:1408], w_q[:, 0:1408], 1408)
            emit_quant(w_ld[:, 1408:2816], w_q[:, 1408:2816], 1408)
            dq = cnt["DVE"]
            for rr in range(3):
                lo = rr * 8
                hi = min(lo + 8, NKI)

                @block.tensor
                def _(t, lo=lo, hi=hi, dq=dq):
                    t.wait_ge(sDVE, cnt["DVE"])
                    t.wait_ge(sDVE, dq)
                    last = None
                    for ic in range(lo, hi):
                        last = t.transpose(ptr[:, (ic - lo) * 128:(ic - lo + 1) * 128],
                                           w_q[:, ic * 128:(ic + 1) * 128], g_idn[:])
                    mark(last, "PE")
                pq = cnt["PE"]

                @block.vector
                def _(v, lo=lo, hi=hi, m=m, pq=pq):
                    v.wait_ge(sPE, pq)
                    mark(v.tensor_copy(
                        w_dsT[:].rearrange("p (ic mm c) -> p ic mm c",
                                           mm=NDT, c=128)[:, lo:hi, m, :],
                        ptr[:, 0:(hi - lo) * 128].rearrange("p (ic c) -> p ic c",
                                                            c=128)), "DVE")

        @block.sync
        def _(sync):
            sync.wait_ge(sDVE, cnt["DVE"])
            jr = JREG["wsp"]
            dma(sync, wguT.rearrange("(kd p) (s f) -> p kd s f",
                                     p=128, s=2)[:, :, bass.ts(jr, 1), :],
                w_gsT[:].rearrange("p (kd f) -> p kd f", f=NWT * 128))
            dma(sync, wdnT.rearrange("(ic p) (s f) -> p ic s f",
                                     p=128, s=2)[:, :, bass.ts(jr, 1), :],
                w_dsT[:].rearrange("p (ic f) -> p ic f", f=NDT * 128))
            sync.wait_ge(qS, cnt["qS"])
            dma(sync, fsh[0:1, bass.ts(jr, 1)], g_jsc[0:1, 2:3])

        barrier()

        if DEBUG_STOP == "W":
            es_ph.close()
            es_aq.close()
            es_ps.close()
            return nc

        es_ph.close()
        es_ps.close()
        es_ps = contextlib.ExitStack()
        psp = lambda n, sh, dt=F32: es_ps.enter_context(nc.psum_tensor(n, sh, dt))
        pa = psp("pa", [128, 512])
        pb = psp("pb", [128, 512])
        pc = psp("pc", [128, 512])
        pd = psp("pd", [128, 512])
        ptr = psp("ptr_a", [128, 1024], BF16)
        es_ph = contextlib.ExitStack()
        sbp = lambda n, sh, dt=F32: es_ph.enter_context(nc.sbuf_tensor(n, sh, dt))
        a_x = sbp("a_x", [128, 4 * 1024])
        a_xn = sbp("a_xn", [128, 4 * 1024])
        a_xnb = sbp("a_xnb", [128, 4 * 1024], BF16)
        a_fm = sbp("a_fm", [128, 8 * 512], BF16)
        a_xps = sbp("a_xps", [96, 512], BF16)
        a_bn = sbp("a_bn", [128, 48])
        a_mv = sbp("a_mv", [128, 8])
        a_mean = sbp("a_mean", [128, 4])
        a_var = sbp("a_var", [128, 4])
        a_rstd = sbp("a_rstd", [128, 4])

        prev_dxn = prev_ptx = prev_pxp = prev_axp = prev_a2 = 0
        for c in range(NC1):
            R0 = c * TC1

            @block.sync
            def _(sync, R0=R0, pw=prev_dxn):
                sync.wait_ge(sDVE, pw)
                dma(sync, a_x[:].rearrange("p (a d) -> p a d", d=DM),
                    x[R0:R0 + TC1, :].rearrange("(a p) d -> p a d", p=128),
                    q="qX")
            ldx = cnt["qX"]
            spw = cnt["qS"]

            @block.vector
            def _(v, ldx=ldx, spw=spw, pa2=prev_a2):
                v.wait_ge(qX, ldx)
                v.wait_ge(qS, spw)
                v.wait_ge(sACT, pa2)
            emit_stats(a_x, 4, a_bn, a_mv, a_mean, a_var, a_rstd)
            a2 = cnt["ACT"]
            prev_a2 = a2

            @block.vector
            def _(v, a2=a2, pptx=prev_ptx):
                v.wait_ge(sACT, a2)
                v.wait_ge(sPE, pptx)
                for a in range(4):
                    v.tensor_scalar(a_xn[:, a * 1024:(a + 1) * 1024],
                                    a_x[:, a * 1024:(a + 1) * 1024],
                                    a_mean[:, a:a + 1], a_rstd[:, a:a + 1],
                                    ALU.subtract, ALU.mult)
                mark(v.tensor_copy(a_xnb[:], a_xn[:]), "DVE")
            dxn = cnt["DVE"]
            prev_dxn = dxn

            for kd in range(NKD):
                @block.tensor
                def _(t, kd=kd, dxn=dxn):
                    t.wait_ge(sDVE, cnt["DVE"])
                    t.wait_ge(sDVE, dxn)
                    last = None
                    for a in range(4):
                        last = t.transpose(ptr[:, a * 128:(a + 1) * 128],
                                           a_xnb[:, a * 1024 + kd * 128: a * 1024 + (kd + 1) * 128],
                                           g_idn[:])
                    mark(last, "PE")
                ptx = cnt["PE"]

                @block.vector
                def _(v, kd=kd, ptx=ptx, ppxp=prev_pxp):
                    v.wait_ge(sPE, ptx)
                    if kd == 0:
                        v.wait_ge(sPE, ppxp)
                    mark(v.tensor_copy(a_fm[:, kd * 512:(kd + 1) * 512], ptr[:, 0:512]),
                         "DVE")
            dfm = cnt["DVE"]
            prev_ptx = cnt["PE"]

            @block.tensor
            def _(t, dfm=dfm, paxp=prev_axp):
                t.wait_ge(sDVE, dfm)
                t.wait_ge(sACT, paxp)
                last = None
                for k in range(NKD):
                    last = t.matmul(pa[0:96, :], g_wxT[:, k * 96:(k + 1) * 96],
                                    a_fm[:, k * 512:(k + 1) * 512],
                                    start=(k == 0), stop=(k == NKD - 1))
                mark(last, "PE")
            pxp = cnt["PE"]
            prev_pxp = pxp

            @block.scalar
            def _(scalar, pxp=pxp, spw=spw):
                scalar.wait_ge(sPE, pxp)
                scalar.wait_ge(qS, spw)
                mark(scalar.activation(out=a_xps[:], in_=pa[0:96, :], func=AF.Copy),
                     "ACT")
            axp = cnt["ACT"]
            prev_axp = axp

            @block.sync
            def _(sync, R0=R0, axp=axp, dfm=dfm):
                sync.wait_ge(sACT, axp)
                dma(sync, xp_scr[:, R0:R0 + TC1], a_xps[:])
                sync.wait_ge(sDVE, dfm)
                dma(sync, xnscan[:, R0:R0 + TC1].rearrange("(g p) t -> p g t", p=128),
                    a_fm[:].rearrange("p (g t) -> p g t", t=512))

        barrier()

        if DEBUG_STOP == "A1":
            es_ph.close()
            es_aq.close()
            es_ps.close()
            return nc

        es_ph.close()
        es_ph = contextlib.ExitStack()
        sbp = lambda n, sh, dt=F32: es_ph.enter_context(nc.sbuf_tensor(n, sh, dt))
        f_x = sbp("f_x", [128, 4 * 1024])
        f_xn = sbp("f_xn", [128, 1024])
        f_xq = sbp("f_xq", [128, 4 * 1024], BF16)
        f_qfm = sbp("f_qfm", [128, 8 * 512], BF16)
        f_lwg = sbp("f_lwg", [128, 2 * 1024], BF16)
        f_lwu = sbp("f_lwu", [128, 2 * 1024], BF16)
        f_sg = sbp("f_sg", [128, 512], BF16)
        f_ub = sbp("f_ub", [128, 512], BF16)
        f_hfm = sbp("f_hfm", [128, 512], BF16)
        f_htm = sbp("f_htm", [128, 4 * IF], BF16)
        f_hqtm = sbp("f_hqtm", [128, 4 * IF], BF16)
        f_hqfm = sbp("f_hqfm", [128, NKI * 512], BF16)
        f_wdn = sbp("f_wdn", [128, 2 * 512], BF16)
        f_g = sbp("f_g", [128, 4 * 1024], BF16)
        f_bn = sbp("f_bn", [128, 48])
        f_mv = sbp("f_mv", [128, 8])
        f_mean = sbp("f_mean", [128, 4])
        f_var = sbp("f_var", [128, 4])
        f_rstd = sbp("f_rstd", [128, 4])

        @block.sync
        def _(sync):
            emit_poll(sync, mybir.EngineType.SP, 0, qF, "W")
            JREG["sp_j"] = jv("sp", sync)

        for c in range(NC2):
            R0 = c * TC2

            @block.sync
            def _(sync, R0=R0):
                dma(sync, f_x[:].rearrange("p (a d) -> p a d", d=DM),
                    x_mine[R0:R0 + TC2, :].rearrange("(a p) d -> p a d", p=128),
                    q="qX")
            ldx = cnt["qX"]

            @block.vector
            def _(v, ldx=ldx):
                v.wait_ge(qX, ldx)
            emit_stats(f_x, 4, f_bn, f_mv, f_mean, f_var, f_rstd)
            a2 = cnt["ACT"]

            for a in range(4):
                @block.vector
                def _(v, a=a, a2=a2):
                    v.wait_ge(sACT, a2)
                    mark(v.tensor_scalar(f_xn[:], f_x[:, a * 1024:(a + 1) * 1024],
                                         f_mean[:, a:a + 1], f_rstd[:, a:a + 1],
                                         ALU.subtract, ALU.mult), "DVE")
                emit_quant(f_xn[:], f_xq[:, a * 1024:(a + 1) * 1024], 1024)
            dq2 = cnt["DVE"]

            for kd in range(NKD):
                @block.tensor
                def _(t, kd=kd, dq2=dq2):
                    t.wait_ge(sDVE, cnt["DVE"])
                    t.wait_ge(sDVE, dq2)
                    last = None
                    for a in range(4):
                        last = t.transpose(ptr[:, a * 128:(a + 1) * 128],
                                           f_xq[:, a * 1024 + kd * 128: a * 1024 + (kd + 1) * 128],
                                           g_idn[:])
                    mark(last, "PE")
                ptx = cnt["PE"]

                @block.vector
                def _(v, kd=kd, ptx=ptx):
                    v.wait_ge(sPE, ptx)
                    mark(v.tensor_copy(f_qfm[:, kd * 512:(kd + 1) * 512], ptr[:, 0:512]),
                         "DVE")
            dqf = cnt["DVE"]

            evac_gate = {}
            evac_up = {}
            mid_evac = {}
            for ic in range(NKI):
                par = ic % 2
                pg = pa if par == 0 else pb
                pu = pc if par == 0 else pd

                @block.sync
                def _(sync, ic=ic, par=par):
                    if ic >= 2:
                        sync.wait_ge(sPE, evac_gate[ic - 2][0])
                    dma(sync, f_lwg[:, par * 1024:(par + 1) * 1024].rearrange("p (k i) -> p k i", i=128),
                        wguT[:, ic * 128:(ic + 1) * 128].rearrange("(k p) i -> p k i", p=128),
                        q="qW")
                    dma(sync, f_lwu[:, par * 1024:(par + 1) * 1024].rearrange("p (k i) -> p k i", i=128),
                        wguT[:, IF + ic * 128:IF + (ic + 1) * 128].rearrange("(k p) i -> p k i", p=128),
                        q="qW")
                lw = cnt["qW"]

                @block.tensor
                def _(t, ic=ic, par=par, lw=lw, dqf=dqf, pg=pg, pu=pu):
                    t.wait_ge(qW, lw)
                    t.wait_ge(sDVE, dqf)
                    if ic >= 2:
                        t.wait_ge(sACT, evac_gate[ic - 2][1])
                    for k in range(NKD):
                        t.matmul(pg[:], f_lwg[:, par * 1024 + k * 128: par * 1024 + (k + 1) * 128],
                                 f_qfm[:, k * 512:(k + 1) * 512],
                                 start=(k == 0), stop=(k == NKD - 1))
                    last = None
                    for k in range(NKD):
                        last = t.matmul(pu[:], f_lwu[:, par * 1024 + k * 128: par * 1024 + (k + 1) * 128],
                                        f_qfm[:, k * 512:(k + 1) * 512],
                                        start=(k == 0), stop=(k == NKD - 1))
                    mark(last, "PE")
                mmc = cnt["PE"]
                evac_gate[ic] = (mmc, None)

                @block.scalar
                def _(scalar, ic=ic, mmc=mmc, pg=pg, pu=pu):
                    scalar.wait_ge(sPE, mmc)
                    scalar.wait_ge(sDVE, mid_evac.get(ic - 1, 0))
                    scalar.activation(out=f_sg[:], in_=pg[:], func=AF.Silu)
                    mark(scalar.activation(out=f_ub[:], in_=pu[:], func=AF.Copy), "ACT")
                evac_gate[ic] = (mmc, cnt["ACT"])

                @block.vector
                def _(v, ic=ic):
                    v.wait_ge(sACT, evac_gate[ic][1])
                    mark(v.tensor_mul(f_hfm[:], f_sg[:], f_ub[:]), "DVE")
                hfm = cnt["DVE"]
                mid_evac[ic] = hfm

                @block.tensor
                def _(t, ic=ic, hfm=hfm):
                    t.wait_ge(sDVE, hfm)
                    t.wait_ge(sDVE, cnt["DVE"])
                    last = None
                    for tt in range(4):
                        last = t.transpose(ptr[:, tt * 128:(tt + 1) * 128],
                                           f_hfm[:, tt * 128:(tt + 1) * 128], g_idn[:])
                    mark(last, "PE")
                ptm = cnt["PE"]

                @block.vector
                def _(v, ic=ic, ptm=ptm):
                    v.wait_ge(sPE, ptm)
                    mark(v.tensor_copy(
                        f_htm[:].rearrange("p (tt k c) -> p tt k c", k=NKI, c=128)[:, :, ic, :],
                        ptr[:, 0:512].rearrange("p (tt c) -> p tt c", c=128)), "DVE")
                mid_evac[ic] = cnt["DVE"]

            for tt in range(4):
                for piece, (off, W) in enumerate([(0, 1408), (1408, 1408)]):
                    emit_quant(f_htm[:, tt * IF + off: tt * IF + off + W],
                               f_hqtm[:, tt * IF + off: tt * IF + off + W], W)
            dhq = cnt["DVE"]

            for ic in range(NKI):
                @block.tensor
                def _(t, ic=ic, dhq=dhq):
                    t.wait_ge(sDVE, dhq)
                    t.wait_ge(sDVE, cnt["DVE"])
                    last = None
                    for tt in range(4):
                        last = t.transpose(ptr[:, tt * 128:(tt + 1) * 128],
                                           f_hqtm[:, tt * IF + ic * 128: tt * IF + (ic + 1) * 128],
                                           g_idn[:])
                    mark(last, "PE")
                ptq = cnt["PE"]

                @block.vector
                def _(v, ic=ic, ptq=ptq):
                    v.wait_ge(sPE, ptq)
                    mark(v.tensor_copy(f_hqfm[:, ic * 512:(ic + 1) * 512], ptr[:, 0:512]),
                         "DVE")
            dqfm = cnt["DVE"]

            PDN = [pa, pb, pc, pd]
            last_dn_evac = cnt["ACT"]
            spw_fg = cnt["qS"]
            for dh in range(2):
                for k in range(NKI):
                    par = k % 2

                    @block.sync
                    def _(sync, dh=dh, k=k, par=par):
                        if k >= 2 or dh > 0:
                            sync.wait_ge(sPE, cnt["PE"])
                        dma(sync, f_wdn[:, par * 512:(par + 1) * 512],
                            wdnT[k * 128:(k + 1) * 128, dh * 512:(dh + 1) * 512],
                            q="qW")
                    lwd = cnt["qW"]

                    @block.tensor
                    def _(t, dh=dh, k=k, par=par, lwd=lwd, dqfm=dqfm,
                          lde=last_dn_evac):
                        t.wait_ge(qW, lwd)
                        t.wait_ge(sDVE, dqfm)
                        if k == 0:
                            t.wait_ge(sACT, lde)
                        last = None
                        for tt in range(4):
                            last = t.matmul(PDN[tt][:],
                                            f_hqfm[:, k * 512 + tt * 128: k * 512 + (tt + 1) * 128],
                                            f_wdn[:, par * 512:(par + 1) * 512],
                                            start=(k == 0), stop=(k == NKI - 1))
                        mark(last, "PE")
                dnm = cnt["PE"]

                @block.scalar
                def _(scalar, dh=dh, dnm=dnm, spw=spw_fg):
                    scalar.wait_ge(sPE, dnm)
                    scalar.wait_ge(qS, spw)
                    last = None
                    for tt in range(4):
                        last = scalar.activation(out=f_g[:, tt * 1024 + dh * 512: tt * 1024 + (dh + 1) * 512],
                                                 in_=PDN[tt][:], func=AF.Silu)
                    mark(last, "ACT")
                last_dn_evac = cnt["ACT"]

            @block.sync
            def _(sync, c=c, lde=last_dn_evac):
                sync.wait_ge(sACT, cnt["ACT"])
                dma(sync, cc2sh.rearrange("(s cc a p) d -> p s cc a d",
                                          s=2, cc=NC2, p=128)
                    [:, bass.ts(JREG["sp_j"], 1), c, :, :],
                    f_g[:].rearrange("p (a d) -> p a d", d=DM))

            barrier()

        @block.sync
        def _(sync):
            sync.wait_ge(qS, cnt["qS"])
            dma(sync, fsh[1:2, bass.ts(JREG["sp_j"], 1)], g_jsc[0:1, 3:4])

        @block.gpsimd
        def _(g):
            JREG["gp_j"] = jv("gp", g)
            g.wait_ge(qS, cnt["qS"])
            emit_poll(g, mybir.EngineType.Pool, 1, qG, "G")
            dma(g, g_my[:], cc2sh[:, bass.ts(JREG["gp_j"], 512)], q="qP")
            dma(g, xn_my[:], xnscan[bass.ts(JREG["gp_j"], 512), :], q="qP")
            g.wait_ge(qP, cnt["qP"])

        barrier()

        if DEBUG_STOP == "A2":
            es_ph.close()
            es_aq.close()
            es_ps.close()
            return nc

        es_ph.close()
        es_aq.close()
        es_ps.close()
        es_ps = contextlib.ExitStack()
        psp = lambda n, sh, dt=F32: es_ps.enter_context(nc.psum_tensor(n, sh, dt))
        pq0 = psp("pq0", [128, 1024])
        pq1 = psp("pq1", [128, 1024])
        ptr = psp("ptr_b", [128, 512], BF16)
        es_ph = contextlib.ExitStack()
        sbp = lambda n, sh, dt=F32: es_ph.enter_context(nc.sbuf_tensor(n, sh, dt))
        b_xn = sbp("b_xn", [128, 4 * 1024], BF16)
        b_xp = sbp("b_xp", [96, 1024], BF16)
        b_xp2 = sbp("b_xp2", [32, 1024], BF16)
        b_xdh = sbp("b_xdh", [128, 8 * 512])
        b_g = sbp("b_g", [128, 8 * 512], BF16)
        b_sig = sbp("b_sig", [128, 1024])
        b_dt = sbp("b_dt", [128, 4 * 1024], BF16)
        b_dx = sbp("b_dx", [128, 4 * 1024], BF16)
        b_B = sbp("b_B", [128, 8 * 1024], BF16)
        b_C = sbp("b_C", [128, 8 * 1024], BF16)
        b_a0 = sbp("b_a0", [128, 8 * SEGW], BF16)
        b_a1 = sbp("b_a1", [128, 8 * SEGW], BF16)
        b_u = sbp("b_u", [128, 8 * SEGW], BF16)
        b_h = sbp("b_h", [128, 8 * SEGW], BF16)
        b_t1 = sbp("b_t1", [128, 4 * 1024], BF16)
        b_t2 = sbp("b_t2", [128, 2 * 1024], BF16)
        b_t3 = sbp("b_t3", [128, 1024])
        b_y = sbp("b_y", [128, 4 * 1024])
        b_s = sbp("b_s", [128, 4 * 1024], BF16)
        b_stm = sbp("b_stm", [128, 512], BF16)
        b_o = sbp("b_o", [128, 512])

        @block.vector
        def _(v):
            v.memset(b_a0[:], 0.0)
            mark(v.memset(b_a1[:], 0.0), "DVE")

        barrier()

        AB = [b_a0, b_a1]
        u3 = b_u.rearrange("p (s l) -> p s l", l=SEGW)
        h3 = b_h.rearrange("p (s l) -> p s l", l=SEGW)
        scan_done = {}
        abuf_free = {}

        for cb in range(NCB):
            R0 = cb * TSB

            @block.sync
            def _(sync, R0=R0):
                dma(sync, b_xn[:].rearrange("p (g t) -> p g t", t=TSB),
                    xn_my[:, R0:R0 + TSB].rearrange("(g p) t -> p g t", p=128))
                dma(sync, b_xp[:], xp_scr[:, R0:R0 + TSB])
                dma(sync, b_xp2[:], xp_scr[64:96, R0:R0 + TSB])
                dma(sync, b_xdh[:].rearrange("p (a d) -> p a d", d=SH),
                    x_dh[R0:R0 + TSB, :].rearrange("(a p) d -> p a d", p=128))
                dma(sync, b_g[:].rearrange("p (a d) -> p a d", d=SH),
                    g_my[R0:R0 + TSB, :].rearrange("(a p) d -> p a d", p=128))
            ldb = cnt["qS"]

            delta_evac = {}
            for g in range(4):
                pdx = pq0 if g % 2 == 0 else pq1

                @block.tensor
                def _(t, g=g, ldb=ldb, pdx=pdx):
                    t.wait_ge(qS, ldb)
                    if g >= 2:
                        t.wait_ge(sACT, delta_evac[g - 2])
                    t.matmul(pdx[:, 0:512], g_wdtT[:, g * 128:(g + 1) * 128],
                             b_xp[0:64, 0:512], start=True, stop=True)
                    mark(t.matmul(pdx[:, 512:1024], g_wdtT[:, g * 128:(g + 1) * 128],
                                  b_xp[0:64, 512:1024], start=True, stop=True), "PE")
                dmm = cnt["PE"]

                @block.scalar
                def _(scalar, g=g, dmm=dmm, pdx=pdx):
                    scalar.wait_ge(sPE, dmm)
                    scalar.activation(out=b_sig[:], in_=pdx[:], func=AF.Sigmoid,
                                      scale=-1.0, bias=g_bdtn[:, g:g + 1])
                    mark(scalar.activation(out=b_dt[:, g * 1024:(g + 1) * 1024],
                                           in_=b_sig[:], func=AF.Ln,
                                           bias=g_tiny[:, 0:1]),
                         "ACT")
                delta_evac[g] = cnt["ACT"]
            dt_done = cnt["ACT"]

            @block.vector
            def _(v, ldb=ldb, dt_done=dt_done):
                v.wait_ge(qS, ldb)
                v.wait_ge(sACT, dt_done)
                last = None
                for g in range(4):
                    last = v.tensor_mul(b_dx[:, g * 1024:(g + 1) * 1024],
                                        b_dt[:, g * 1024:(g + 1) * 1024],
                                        b_xn[:, g * 1024:(g + 1) * 1024])
                mark(last, "DVE")
            dx_done = cnt["DVE"]

            it = 0
            bc_psum_free = {id(pq0): ("ACT", delta_evac[2]),
                            id(pq1): ("ACT", delta_evac[3])}
            bci = 0
            for hh in range(2):
                for n in range(8):
                    nn = hh * 8 + n
                    for which, dst in ((0, b_B), (1, b_C)):
                        pq_ = pq0 if bci % 2 == 0 else pq1
                        prevkind, prevfree = bc_psum_free[id(pq_)]

                        @block.tensor
                        def _(t, nn=nn, which=which, pq_=pq_, ldb=ldb,
                              prevfree=prevfree, prevkind=prevkind):
                            t.wait_ge(qS, ldb)
                            t.wait_ge(SEMS[prevkind], prevfree)
                            sel = g_sel[:, (which * 16 + nn) * 128:(which * 16 + nn + 1) * 128]
                            t.matmul(pq_[:, 0:512], sel, b_xp2[:, 0:512],
                                     start=True, stop=True)
                            mark(t.matmul(pq_[:, 512:1024], sel, b_xp2[:, 512:1024],
                                          start=True, stop=True), "PE")
                        bmm = cnt["PE"]

                        if bci % 4 < 2:
                            @block.vector
                            def _(v, n=n, dst=dst, bmm=bmm, pq_=pq_):
                                v.wait_ge(sPE, bmm)
                                mark(v.tensor_copy(dst[:, n * 1024:(n + 1) * 1024],
                                                   pq_[:]), "DVE")
                            bc_psum_free[id(pq_)] = ("DVE", cnt["DVE"])
                        else:
                            @block.scalar
                            def _(scalar, n=n, dst=dst, bmm=bmm, pq_=pq_):
                                scalar.wait_ge(sPE, bmm)
                                mark(scalar.activation(
                                    out=dst[:, n * 1024:(n + 1) * 1024],
                                    in_=pq_[:], func=AF.Copy), "ACT")
                            bc_psum_free[id(pq_)] = ("ACT", cnt["ACT"])
                        bci += 1
                bc_done_d = cnt["DVE"]
                bc_done_a = cnt["ACT"]

                for g in range(4):
                    par = it % 2
                    abuf = AB[par]

                    @block.scalar
                    def _(scalar, g=g, hh=hh, par=par, abuf=abuf):
                        prev = abuf_free.get(par)
                        if prev is not None:
                            scalar.wait_ge(sDVE, prev)
                        scalar.wait_ge(sACT, dt_done)
                        last = None
                        for n in range(8):
                            last = scalar.activation(
                                out=abuf[:, n * SEGW + 1: n * SEGW + 1 + TSB],
                                in_=b_dt[:, g * 1024:(g + 1) * 1024], func=AF.Exp,
                                scale=g_expA[:, g * 16 + hh * 8 + n: g * 16 + hh * 8 + n + 1])
                        mark(last, "ACT")
                    aw = cnt["ACT"]

                    @block.vector
                    def _(v, g=g, hh=hh, par=par, abuf=abuf, aw=aw, dx_done=dx_done,
                          bc_done_d=bc_done_d, bc_done_a=bc_done_a):
                        v.wait_ge(sACT, max(aw, bc_done_a))
                        v.wait_ge(sDVE, max(dx_done, bc_done_d))
                        base = g * 16 + hh * 8
                        dxg = b_dx[:, g * 1024:(g + 1) * 1024] \
                            .rearrange("p (o t) -> p o t", o=1).broadcast_to([128, 8, TSB])
                        BH = b_B[:].rearrange("p (s t) -> p s t", t=1024)
                        CH = b_C[:].rearrange("p (s t) -> p s t", t=1024)
                        v.tensor_tensor(u3[:, :, 1:], dxg, BH, ALU.mult)
                        mark(v.tensor_copy(u3[:, :, 0:1],
                                           g_state[:, base:base + 8].rearrange("p (s o) -> p s o", o=1)),
                             "DVE")
                        v.wait_ge(sDVE, cnt["DVE"])
                        mark(v.tensor_tensor_scan(b_h[:], abuf[:], b_u[:],
                                                  g_zero[:, 0:1], ALU.mult, ALU.add),
                             "DVE")
                        abuf_free[par] = cnt["DVE"]
                        v.wait_ge(sDVE, cnt["DVE"])
                        v.tensor_copy(g_state[:, base:base + 8].rearrange("p (s o) -> p s o", o=1),
                                      h3[:, :, TSB:TSB + 1])
                        v.tensor_tensor(u3[:, :, 1:], h3[:, :, 1:], CH, ALU.mult)
                        v.tensor_tensor(b_t1[:].rearrange("p (s t) -> p s t", t=1024),
                                        u3[:, 0:4, 1:], u3[:, 4:8, 1:], ALU.add)
                        v.tensor_add(b_t2[:], b_t1[:, 0:2048], b_t1[:, 2048:4096])
                        if hh == 0:
                            mark(v.tensor_add(b_y[:, g * 1024:(g + 1) * 1024],
                                              b_t2[:, 0:1024], b_t2[:, 1024:2048]),
                                 "DVE")
                        else:
                            v.tensor_add(b_t3[:], b_t2[:, 0:1024], b_t2[:, 1024:2048])
                            mark(v.tensor_add(b_y[:, g * 1024:(g + 1) * 1024],
                                              b_y[:, g * 1024:(g + 1) * 1024],
                                              b_t3[:]), "DVE")
                    it += 1

            @block.vector
            def _(v):
                last = None
                for g in range(4):
                    last = v.scalar_tensor_tensor(b_s[:, g * 1024:(g + 1) * 1024],
                                                  b_xn[:, g * 1024:(g + 1) * 1024],
                                                  g_dvec[:, g:g + 1],
                                                  b_y[:, g * 1024:(g + 1) * 1024],
                                                  ALU.mult, ALU.subtract)
                mark(last, "DVE")
            sdone = cnt["DVE"]

            for tt in range(8):
                @block.tensor
                def _(t, tt=tt, sdone=sdone):
                    t.wait_ge(sDVE, sdone)
                    t.wait_ge(sDVE, cnt["DVE"])
                    last = None
                    for g in range(4):
                        last = t.transpose(ptr[:, g * 128:(g + 1) * 128],
                                           b_s[:, g * 1024 + tt * 128: g * 1024 + (tt + 1) * 128],
                                           g_idn[:])
                    mark(last, "PE")
                pts = cnt["PE"]

                @block.vector
                def _(v, tt=tt, pts=pts, pool_prev=cnt["POOL"]):
                    v.wait_ge(sPE, pts)
                    v.wait_ge(sPOOL, pool_prev)
                    mark(v.tensor_copy(b_stm[:], ptr[:, 0:512]), "DVE")
                stmd = cnt["DVE"]

                @block.gpsimd
                def _(g, tt=tt, stmd=stmd, ldb=ldb):
                    g.wait_ge(sDVE, stmd)
                    g.wait_ge(qS, ldb)
                    g.wait_ge(qP, cnt["qP"])
                    g.tensor_tensor(b_o[:], b_stm[:], b_g[:, tt * 512:(tt + 1) * 512],
                                    ALU.mult).then_inc(sPOOL, 1)
                    cnt["POOL"] += 1
                    g.wait_ge(sPOOL, cnt["POOL"])
                    mark(g.tensor_add(b_o[:], b_o[:], b_xdh[:, tt * 512:(tt + 1) * 512]),
                         "POOL")
                @block.gpsimd
                def _(g, tt=tt, R0=R0):
                    dma(g, o[R0 + tt * 128: R0 + (tt + 1) * 128, :], b_o[:], q="qP")

            barrier()

        @block.sync
        def _(sync):
            sync.wait_ge(qP, cnt["qP"])
            sync.wait_ge(qS, cnt["qS"])

        es_ph.close()
        es_ps.close()

    return nc


def kernel(L0_attn_x, L0_attn_W_xproj, L0_attn_W_dt, L0_attn_b_dt, L0_attn_A_log,
           L0_attn_D, L0_ffn_w_gate, L0_ffn_w_up, L0_ffn_w_down):
    x = np.ascontiguousarray(np.asarray(L0_attn_x, dtype=np.float32))
    W_xproj = np.asarray(L0_attn_W_xproj, dtype=np.float32)
    W_dt = np.asarray(L0_attn_W_dt, dtype=np.float32)
    b_dt = np.asarray(L0_attn_b_dt, dtype=np.float32)
    A_log = np.asarray(L0_attn_A_log, dtype=np.float32)
    D = np.asarray(L0_attn_D, dtype=np.float32)
    w_gate = np.asarray(L0_ffn_w_gate, dtype=np.float32)
    w_up = np.asarray(L0_ffn_w_up, dtype=np.float32)
    w_down = np.asarray(L0_ffn_w_down, dtype=np.float32)

    wxT = np.ascontiguousarray(W_xproj.T)
    sel = np.zeros((32, 32 * 128), np.float32)
    for n in range(32):
        sel[n, n * 128:(n + 1) * 128] = 1.0
    idn = np.eye(128, dtype=np.float32)

    if "nc" not in _NC_CACHE:
        _NC_CACHE["nc"] = build()
    nc = _NC_CACHE["nc"]

    rtok = np.random.randint(1 << 20, (1 << 30), size=2).astype(np.int32)

    in_maps = []
    for i in range(8):
        b, j = i // 2, i % 2
        sl = slice(j * SH, (j + 1) * SH)
        in_maps.append(dict(
            x=x[b],
            x_dh=np.ascontiguousarray(x[b][:, sl]),
            x_mine=np.ascontiguousarray(x[b][j * TOK:(j + 1) * TOK]),
            wxT=wxT,
            wdtT_h=np.ascontiguousarray(W_dt.T[:, sl]),
            bdt_h=np.ascontiguousarray(b_dt[sl].reshape(4, 128).T),
            alog_h=np.ascontiguousarray(
                A_log[sl].reshape(4, 128, 16).transpose(1, 0, 2).reshape(128, 64)),
            dvec_h=np.ascontiguousarray(D[sl].reshape(4, 128).T),
            bc_sel=sel,
            idn=idn,
            wshare=np.ascontiguousarray(w_gate if j == 0 else w_up),
            wdshare=np.ascontiguousarray(w_down[j * 512:(j + 1) * 512]),
            jsc=np.array([[j, 1 - j, rtok[0], rtok[1]]], np.int32),
        ))
    res = run_bass_kernel_spmd(nc, in_maps, core_ids=list(range(8)))
    out = np.empty((B_, S, DM), np.float32)
    for i in range(8):
        b, j = i // 2, i % 2
        out[b, :, j * SH:(j + 1) * SH] = np.asarray(res.results[i]["o"])
    return out
